# version 01
# baseline (speedup 1.0000x reference)
import numpy as np
import ml_dtypes
from contextlib import ExitStack
import concourse.bass as bass
import concourse.mybir as mybir
from concourse.bass_utils import run_bass_kernel_spmd
from concourse.alu_op_type import AluOpType as ALU

F32 = mybir.dt.float32
BF16 = mybir.dt.bfloat16
AF = mybir.ActivationFunctionType
AX = mybir.AxisListType.X
BF = ml_dtypes.bfloat16

D = 2048
T = 8192
NCORE = 8
TC = T // NCORE
NCH = D // 128
EPS = 1e-5
RG = [list(range(NCORE))]


class Sem:
    def __init__(self, es, nc, name):
        self.h = es.enter_context(nc.semaphore(name))
        self.n = 0

    def inc(self, instr, k=1):
        instr.then_inc(self.h, k)
        self.n += k
        return self.n


def host_consts():
    oh8 = np.zeros((8, 1024), BF)
    for e in range(8):
        oh8[e, e * 128:(e + 1) * 128] = 1
    return {
        "c_ones_c1": np.ones((128, 1), BF),
        "c_ones_1r": np.ones((1, 128), BF),
        "c_oh8": oh8,
        "c_ident": np.eye(128, dtype=np.float32),
        "c_ones_r512": np.ones((1, 512), BF),
        "c_swap": np.roll(np.eye(128), 64, axis=1).astype(BF),
        "c_h128": (_hadamard(128) * 2.0 ** -7.5).astype(BF),
    }


def _hadamard(n):
    h = np.array([[1.0]])
    while h.shape[0] < n:
        h = np.block([[h, h], [h, -h]])
    return h


def host_masks():
    m = np.zeros((128, 4 * 512), BF)
    for o in range(4):
        k = np.arange(128)[:, None] + o * 128
        q = np.arange(512)[None, :]
        m[:, o * 512:(o + 1) * 512] = (k <= q).astype(BF)
    return m


def host_rope():
    half = 64
    pos = np.arange(T, dtype=np.float64)[None, :]
    freqs = (10000.0 ** (-np.arange(half, dtype=np.float64) / half))[:, None]
    ang = freqs * pos
    c, s = np.cos(ang), np.sin(ang)
    rc = np.concatenate([c, c], 0).astype(BF)
    rs = np.concatenate([-s, s], 0).astype(BF)
    return rc, rs


class Builder:
    def __init__(self, race_check=True):
        self.nc = bass.Bass("TRN2", target_bir_lowering=False, debug=False,
                            detect_race_conditions=race_check)
        self.es = ExitStack()

    def dram_in(self, name, shape, dtype=F32):
        return self.nc.dram_tensor(name, list(shape), dtype, kind="ExternalInput").ap()

    def dram_out(self, name, shape, dtype=F32):
        return self.nc.dram_tensor(name, list(shape), dtype, kind="ExternalOutput").ap()

    def dram_int(self, name, shape, dtype=F32, addr_space="Local"):
        return self.nc.dram_tensor(name, list(shape), dtype, kind="Internal",
                                   addr_space=addr_space).ap()

    def setup(self):
        nc, es = self.nc, self.es
        self.xln = es.enter_context(nc.sbuf_tensor("xln", [128, NCH * TC], BF16))
        self.ones_c1 = es.enter_context(nc.sbuf_tensor("ones_c1", [128, 1], BF16))
        self.ones_1r = es.enter_context(nc.sbuf_tensor("ones_1r", [1, 128], BF16))
        self.oh8 = es.enter_context(nc.sbuf_tensor("oh8", [8, 1024], BF16))
        self.ident = es.enter_context(nc.sbuf_tensor("ident", [128, 128], F32))
        self.rowA = es.enter_context(nc.sbuf_tensor("rowA", [1, TC], F32))
        self.rowB = es.enter_context(nc.sbuf_tensor("rowB", [1, TC], F32))
        self.rowC = es.enter_context(nc.sbuf_tensor("rowC", [1, TC], F32))
        self.rbf = es.enter_context(nc.sbuf_tensor("rbf", [1, TC], BF16))
        self.mrbf = es.enter_context(nc.sbuf_tensor("mrbf", [1, TC], BF16))
        self.t1 = es.enter_context(nc.sbuf_tensor("t1", [128, 512], F32))
        self.onesr512 = es.enter_context(nc.sbuf_tensor("onesr512", [1, 512], BF16))
        self.swp = es.enter_context(nc.sbuf_tensor("swp", [128, 128], BF16))
        self.h128 = es.enter_context(nc.sbuf_tensor("h128", [128, 128], BF16))
        self.sD = Sem(es, nc, "sD")
        self.sW = Sem(es, nc, "sW")
        self.sT = Sem(es, nc, "sT")
        self.sV = Sem(es, nc, "sV")
        self.sS = Sem(es, nc, "sS")
        self.sO = Sem(es, nc, "sO")
        self.sWw = [Sem(es, nc, f"sWw{i}") for i in range(4)]
        self.sVq = Sem(es, nc, "sVq")
        self.sSq = Sem(es, nc, "sSq")
        self._vlast = (None, 0)
        self._slast = (None, 0)
        self.d_consts = {n: self.dram_in(n, a.shape, F32 if a.dtype == np.float32 else BF16)
                         for n, a in host_consts().items()}
        self._xres_n = 0
        self.xres_es = ExitStack()
        self.xres = self.xres_es.enter_context(nc.sbuf_tensor("xres", [128, NCH * TC], F32))
        self.block = self.es.enter_context(self.nc.Block())

    def finish(self):
        if self.xres is not None:
            self.xres_es.close()
            self.xres = None
        self.es.close()

    def free_xres(self):
        self.xres_es.close()
        self.xres = None

    def alloc_xres(self):
        self._xres_n += 1
        self.xres_es = ExitStack()
        self.xres = self.xres_es.enter_context(
            self.nc.sbuf_tensor(f"xres{self._xres_n}", [128, NCH * TC], F32))

    def V(self, v, name, *a, _sem=None, **k):
        if self._vlast[1]:
            v.wait_ge(self._vlast[0], self._vlast[1])
        ins = getattr(v, name)(*a, **k)
        sem = _sem if _sem is not None else self.sVq
        sem.inc(ins)
        self._vlast = (sem.h, sem.n)
        return ins

    def S(self, s, name, *a, _sem=None, **k):
        if self._slast[1]:
            s.wait_ge(self._slast[0], self._slast[1])
        ins = getattr(s, name)(*a, **k)
        sem = _sem if _sem is not None else self.sSq
        sem.inc(ins)
        self._slast = (sem.h, sem.n)
        return ins

    def load_consts(self):
        sD = self.sD

        def f(sync):
            sD.inc(sync.dma_start(out=self.ones_c1[:], in_=self.d_consts["c_ones_c1"]), 16)
            sD.inc(sync.dma_start(out=self.ones_1r[:], in_=self.d_consts["c_ones_1r"]), 16)
            sD.inc(sync.dma_start(out=self.oh8[:], in_=self.d_consts["c_oh8"]), 16)
            sD.inc(sync.dma_start(out=self.ident[:], in_=self.d_consts["c_ident"]), 16)
            sD.inc(sync.dma_start(out=self.onesr512[:], in_=self.d_consts["c_ones_r512"]), 16)
            sD.inc(sync.dma_start(out=self.swp[:], in_=self.d_consts["c_swap"]), 16)
            sD.inc(sync.dma_start(out=self.h128[:], in_=self.d_consts["c_h128"]), 16)
        self.block.sync(f)

    def load_x(self, x_dram):
        sD = self.sD

        def f(sync):
            sD.inc(sync.dma_start(
                out=self.xres[:].rearrange("p (c t) -> p c t", c=NCH),
                in_=x_dram.rearrange("(c p) t -> p c t", p=128)), 16)
        self.block.sync(f)

    def store_x(self, x_out, wait):
        sO = self.sO

        def f(sync):
            for s, v in wait:
                sync.wait_ge(s.h, v)
            sO.inc(sync.dma_start(
                out=x_out.rearrange("(c p) t -> p c t", p=128),
                in_=self.xres[:].rearrange("p (c t) -> p c t", c=NCH)), 16)
            sync.wait_ge(sO.h, sO.n)
        self.block.sync(f)

    def emit_ln(self, scratch_bf, wait_sems=()):
        nc, b = self.nc, self.block
        sT, sV, sS = self.sT, self.sV, self.sS
        es2 = ExitStack()
        ps_sum = [es2.enter_context(nc.psum_tensor(f"ln_s{h}", [1, 512], F32)) for h in range(2)]
        ps_sq = [es2.enter_context(nc.psum_tensor(f"ln_q{h}", [1, 512], F32)) for h in range(2)]
        ps_rb = [es2.enter_context(nc.psum_tensor(f"ln_rb{h}", [128, 512], F32)) for h in range(2)]
        ps_mrb = [es2.enter_context(nc.psum_tensor(f"ln_mrb{h}", [128, 512], F32)) for h in range(2)]
        vbase = sV.n

        def v_cast(v):
            for s, val in wait_sems:
                v.wait_ge(s.h, val)
            for c in range(NCH):
                sl = slice(c * TC, (c + 1) * TC)
                self.V(v, "tensor_copy", self.xln[:, sl], self.xres[:, sl])
                self.V(v, "tensor_mul", scratch_bf[:, sl], self.xln[:, sl], self.xln[:, sl], _sem=sV)
        b.vector(v_cast)

        def t_stats(t):
            ins = None
            for c in range(NCH):
                t.wait_ge(sV.h, vbase + c + 1)
                for h in range(2):
                    sl = slice(c * TC + h * 512, c * TC + (h + 1) * 512)
                    t.matmul(ps_sum[h][:], self.ones_c1[:], self.xln[:, sl],
                             start=(c == 0), stop=(c == NCH - 1))
                    ins = t.matmul(ps_sq[h][:], self.ones_c1[:], scratch_bf[:, sl],
                                   start=(c == 0), stop=(c == NCH - 1))
            sT.inc(ins)
        b.tensor(t_stats)
        tstats = sT.n

        def v_fin(v):
            v.wait_ge(sT.h, tstats)
            for h in range(2):
                sl = slice(h * 512, (h + 1) * 512)
                self.V(v, "tensor_scalar_mul", self.rowA[:, sl], ps_sum[h][:], 1.0 / D)
                self.V(v, "tensor_scalar_mul", self.rowB[:, sl], ps_sq[h][:], 1.0 / D)
            self.V(v, "tensor_mul", self.rowC[:], self.rowA[:], self.rowA[:])
            self.V(v, "tensor_sub", self.rowB[:], self.rowB[:], self.rowC[:])
            self.V(v, "tensor_scalar_add", self.rowB[:], self.rowB[:], EPS, _sem=sV)
        b.vector(v_fin)
        vfin = sV.n

        def s_sqrt(s):
            s.wait_ge(sV.h, vfin)
            self.S(s, "activation", self.rowC[:], self.rowB[:], AF.Sqrt, _sem=sS)
        b.scalar(s_sqrt)
        ssq = sS.n

        def v_fin2(v):
            v.wait_ge(sS.h, ssq)
            self.V(v, "reciprocal", self.rowB[:], self.rowC[:])
            self.V(v, "tensor_mul", self.rowA[:], self.rowA[:], self.rowB[:])
            self.V(v, "tensor_copy", self.rbf[:], self.rowB[:])
            self.V(v, "tensor_copy", self.mrbf[:], self.rowA[:], _sem=sV)
        b.vector(v_fin2)
        vfin2 = sV.n

        def t_bc(t):
            t.wait_ge(sV.h, vfin2)
            ins = None
            for h in range(2):
                sl = slice(h * 512, (h + 1) * 512)
                t.matmul(ps_rb[h][:], self.ones_1r[:], self.rbf[:, sl], start=True, stop=True)
                ins = t.matmul(ps_mrb[h][:], self.ones_1r[:], self.mrbf[:, sl], start=True, stop=True)
            sT.inc(ins)
        b.tensor(t_bc)
        tbc = sT.n

        def v_norm(v):
            v.wait_ge(sT.h, tbc)
            ins = None
            for c in range(NCH):
                for h in range(2):
                    sl = slice(c * TC + h * 512, c * TC + (h + 1) * 512)
                    self.V(v, "tensor_mul", self.t1[:], self.xres[:, sl], ps_rb[h][:])
                    self.V(v, "tensor_sub", self.xln[:, sl], self.t1[:], ps_mrb[h][:],
                           _sem=sV if (c == NCH - 1 and h == 1) else None)
        b.vector(v_norm)
        es2.close()
        return sV.n

    def emit_moe(self, wr_d, wg_d, wu_d, wd_d, hbuf, xln_ready, n_experts=8):
        nc, b = self.nc, self.block
        sT, sV, sS, sW = self.sT, self.sV, self.sS, self.sW
        es2 = ExitStack()
        wb = es2.enter_context(nc.sbuf_tensor("moe_wb", [128, 8 * TC], BF16, side="right"))
        wrb = es2.enter_context(nc.sbuf_tensor("moe_wr", [128, NCH * 8], BF16, side="right"))
        sg = [es2.enter_context(nc.sbuf_tensor(f"moe_sg{i}", [128, TC], BF16, side="right")) for i in range(2)]
        uw = es2.enter_context(nc.sbuf_tensor("moe_uw", [128, TC], BF16, side="right"))
        wbufs = [es2.enter_context(nc.sbuf_tensor(f"moe_w{i}", [128, NCH * 256], BF16, side="right"))
                 for i in range(4)]
        probs = es2.enter_context(nc.sbuf_tensor("moe_pr", [128, 64], F32, side="right"))
        nmax = es2.enter_context(nc.sbuf_tensor("moe_nm", [128, 8], F32, side="right"))
        tiny = es2.enter_context(nc.sbuf_tensor("moe_ty", [128, 24], F32, side="right"))
        w_tm = es2.enter_context(nc.sbuf_tensor("moe_wtm", [128, 64], F32, side="right"))
        wfm = es2.enter_context(nc.sbuf_tensor("moe_wfm", [8, 1024], BF16, side="right"))

        def g_wr(g):
            g.wait_ge(sV.h, xln_ready)
            sW.inc(g.dma_start(out=wrb[:].rearrange("p (c f) -> p c f", c=NCH),
                               in_=wr_d.rearrange("(c p) f -> p c f", p=128)), 16)
        b.gpsimd(g_wr)
        wr_ld = sW.n

        es_r = ExitStack()
        ps_rt = es_r.enter_context(nc.psum_tensor("moe_psrt", [128, 64], F32))

        def t_router(t):
            t.wait_ge(sV.h, xln_ready)
            t.wait_ge(sW.h, wr_ld)
            ins = None
            for ts in range(8):
                for m in range(NCH):
                    ins = t.matmul(ps_rt[:, ts * 8:(ts + 1) * 8],
                                   self.xln[:, m * TC + ts * 128: m * TC + (ts + 1) * 128],
                                   wrb[:, m * 8:(m + 1) * 8],
                                   start=(m == 0), stop=(m == NCH - 1))
            sT.inc(ins)
        b.tensor(t_router)
        t_rt = sT.n

        def v_rmax(v):
            v.wait_ge(sT.h, t_rt)
            ins = None
            for ts in range(8):
                self.V(v, "reduce_max", nmax[:, ts:ts + 1], ps_rt[:, ts * 8:(ts + 1) * 8],
                       axis=AX, negate=True, _sem=sV if ts == 7 else None)
        b.vector(v_rmax)
        v_mx = sV.n

        def s_rexp(s):
            s.wait_ge(sV.h, v_mx)
            ins = None
            for ts in range(8):
                self.S(s, "activation", probs[:, ts * 8:(ts + 1) * 8], ps_rt[:, ts * 8:(ts + 1) * 8],
                       AF.Exp, bias=nmax[:, ts:ts + 1], _sem=sS if ts == 7 else None)
        b.scalar(s_rexp)
        s_ex = sS.n

        def v_top2(v):
            v.wait_ge(sS.h, s_ex)
            eq, pm = tiny[:, 8:16], tiny[:, 16:24]
            c_sum, c_m1, c_m2, c_d = (tiny[:, 0:1], tiny[:, 1:2], tiny[:, 2:3], tiny[:, 3:4])
            ins = None
            for ts in range(8):
                p8 = probs[:, ts * 8:(ts + 1) * 8]
                self.V(v, "reduce_sum", c_sum, p8, axis=AX)
                self.V(v, "reciprocal", c_d, c_sum)
                self.V(v, "tensor_scalar_mul", p8, p8, c_d)
                self.V(v, "reduce_max", c_m1, p8, axis=AX)
                self.V(v, "tensor_single_scalar", eq, p8, c_m1, ALU.is_equal)
                self.V(v, "tensor_mul", pm, eq, p8)
                self.V(v, "tensor_sub", pm, p8, pm)
                self.V(v, "reduce_max", c_m2, pm, axis=AX)
                self.V(v, "tensor_add", c_d, c_m1, c_m2)
                self.V(v, "reciprocal", c_d, c_d)
                self.V(v, "tensor_scalar_mul", c_d, c_d, 0.5)
                self.V(v, "tensor_single_scalar", eq, p8, c_m2, ALU.is_ge)
                self.V(v, "tensor_mul", pm, p8, eq)
                self.V(v, "tensor_scalar_mul", w_tm[:, ts * 8:(ts + 1) * 8], pm, c_d,
                       _sem=sV if ts == 7 else None)
        b.vector(v_top2)
        v_t2 = sV.n
        es_r.close()

        es_w = ExitStack()
        ps_wt = es_w.enter_context(nc.psum_tensor("moe_pswt", [8, 1024], F32))

        def t_wt(t):
            t.wait_ge(sV.h, v_t2)
            ins = None
            for ts in range(8):
                ins = t.transpose(ps_wt[:, ts * 128:(ts + 1) * 128],
                                  w_tm[:, ts * 8:(ts + 1) * 8], self.ident[:])
            sT.inc(ins)
        b.tensor(t_wt)
        t_tr = sT.n

        def v_wfm(v):
            v.wait_ge(sT.h, t_tr)
            self.V(v, "tensor_copy", wfm[:], ps_wt[:], _sem=sV)
        b.vector(v_wfm)
        v_wf = sV.n
        es_w.close()

        es_wb = ExitStack()
        ps_wb = [es_wb.enter_context(nc.psum_tensor(f"moe_pswb{i}", [128, TC], F32)) for i in range(2)]
        tb0, vb0 = sT.n, sV.n

        def t_wb(t):
            t.wait_ge(sV.h, v_wf)
            for e in range(8):
                if e >= 2:
                    t.wait_ge(sV.h, vb0 + e - 1)
                ins = None
                for h in range(2):
                    ins = t.matmul(ps_wb[e % 2][:, h * 512:(h + 1) * 512],
                                   self.oh8[:, e * 128:(e + 1) * 128],
                                   wfm[:, h * 512:(h + 1) * 512], start=True, stop=True)
                sT.inc(ins)
        b.tensor(t_wb)

        def v_wb(v):
            for e in range(8):
                v.wait_ge(sT.h, tb0 + e + 1)
                self.V(v, "tensor_copy", wb[:, e * TC:(e + 1) * TC], ps_wb[e % 2][:], _sem=sV)
        b.vector(v_wb)
        wb_ready = sV.n
        es_wb.close()

        ps_g = [es2.enter_context(nc.psum_tensor(f"moe_pg{i}", [128, TC], F32)) for i in range(2)]
        ps_u = [es2.enter_context(nc.psum_tensor(f"moe_pu{i}", [128, TC], F32)) for i in range(2)]

        sT0 = sT.n
        sS0 = sS.n
        sV0 = sV.n

        marks = {}
        sWw = self.sWw

        def g_loads(g):
            for e in range(n_experts):
                for grp in range(8):
                    if e > 0 or grp >= 2:
                        if grp >= 2:
                            need = e * 32 + 2 * (grp - 1)
                        else:
                            need = (e - 1) * 32 + 16 + 14 + 2 * grp
                        g.wait_ge(sT.h, sT0 + need)
                    cols = slice(grp * 256, (grp + 1) * 256)
                    p = grp % 2
                    sWw[p].inc(g.dma_start(
                        out=wbufs[p][:].rearrange("p (c f) -> p c f", c=NCH),
                        in_=wg_d[e, :, cols].rearrange("(c p) f -> p c f", p=128)), 16)
                    sWw[2 + p].inc(g.dma_start(
                        out=wbufs[2 + p][:].rearrange("p (c f) -> p c f", c=NCH),
                        in_=wu_d[e, :, cols].rearrange("(c p) f -> p c f", p=128)), 16)
                    marks[("up", e, grp)] = [(sWw[p], sWw[p].n), (sWw[2 + p], sWw[2 + p].n)]
                for grp in range(8):
                    if grp < 2:
                        need = e * 32 + 14 + 2 * grp
                    else:
                        need = e * 32 + 16 + 2 * (grp - 1)
                    g.wait_ge(sT.h, sT0 + need)
                    cols = slice(grp * 256, (grp + 1) * 256)
                    p = grp % 2
                    sWw[p].inc(g.dma_start(
                        out=wbufs[p][:].rearrange("p (c f) -> p c f", c=NCH),
                        in_=wd_d[e, :, cols].rearrange("(c p) f -> p c f", p=128)), 16)
                    marks[("dn", e, grp)] = [(sWw[p], sWw[p].n)]
        b.gpsimd(g_loads)

        def t_experts(t):
            t.wait_ge(sV.h, wb_ready)
            for e in range(n_experts):
                for fs in range(NCH):
                    grp = fs // 2
                    if fs % 2 == 0:
                        for sm, val in marks[("up", e, grp)]:
                            t.wait_ge(sm.h, val)
                    fs_idx = e * 16 + fs
                    if fs >= 2:
                        t.wait_ge(sS.h, sS0 + e * 16 + fs - 1)
                        t.wait_ge(sV.h, sV0 + e * 32 + fs - 1)
                    elif e > 0:
                        t.wait_ge(sV.h, sV0 + e * 32)
                    wcol = (fs % 2) * 128
                    ins = None
                    for m in range(NCH):
                        lg = wbufs[grp % 2][:, m * 256 + wcol: m * 256 + wcol + 128]
                        lu = wbufs[2 + grp % 2][:, m * 256 + wcol: m * 256 + wcol + 128]
                        rx = self.xln[:, m * TC:(m + 1) * TC]
                        for h in range(2):
                            t.matmul(ps_g[fs % 2][:, h * 512:(h + 1) * 512], lg,
                                     rx[:, h * 512:(h + 1) * 512],
                                     start=(m == 0), stop=(m == NCH - 1))
                        for h in range(2):
                            ins = t.matmul(ps_u[fs % 2][:, h * 512:(h + 1) * 512], lu,
                                           rx[:, h * 512:(h + 1) * 512],
                                           start=(m == 0), stop=(m == NCH - 1))
                    sT.inc(ins)
                for ds in range(NCH):
                    grp = ds // 2
                    if ds % 2 == 0:
                        for sm, val in marks[("dn", e, grp)]:
                            t.wait_ge(sm.h, val)
                    if ds == 0:
                        t.wait_ge(sV.h, sV0 + e * 32 + 16)
                        t.wait_ge(sS.h, sS0 + (e + 1) * 16)
                    if ds >= 2:
                        t.wait_ge(sV.h, sV0 + e * 32 + 16 + ds - 1)
                    wcol = (ds % 2) * 128
                    ins = None
                    for fc in range(NCH):
                        lw = wbufs[grp % 2][:, fc * 256 + wcol: fc * 256 + wcol + 128]
                        for h in range(2):
                            ins = t.matmul(ps_g[ds % 2][:, h * 512:(h + 1) * 512], lw,
                                           hbuf[:, fc * TC + h * 512: fc * TC + (h + 1) * 512],
                                           start=(fc == 0), stop=(fc == NCH - 1))
                    sT.inc(ins)
        b.tensor(t_experts)

        def s_silu(s):
            for e in range(n_experts):
                for fs in range(NCH):
                    fs_idx = e * 16 + fs
                    s.wait_ge(sT.h, sT0 + e * 32 + fs + 1)
                    if fs_idx >= 2:
                        if fs >= 2:
                            s.wait_ge(sV.h, sV0 + e * 32 + fs - 1)
                        else:
                            s.wait_ge(sV.h, sV0 + (e - 1) * 32 + 14 + fs + 1)
                    self.S(s, "activation", sg[fs % 2][:], ps_g[fs % 2][:], AF.Silu, _sem=sS)
        b.scalar(s_silu)

        def v_evac(v):
            for e in range(n_experts):
                for fs in range(NCH):
                    v.wait_ge(sS.h, sS0 + e * 16 + fs + 1)
                    self.V(v, "tensor_mul", uw[:], ps_u[fs % 2][:], wb[:, e * TC:(e + 1) * TC])
                    self.V(v, "tensor_mul", hbuf[:, fs * TC:(fs + 1) * TC], sg[fs % 2][:], uw[:], _sem=sV)
                for ds in range(NCH):
                    v.wait_ge(sT.h, sT0 + e * 32 + 16 + ds + 1)
                    sl = slice(ds * TC, (ds + 1) * TC)
                    self.V(v, "tensor_tensor", self.xres[:, sl], self.xres[:, sl],
                                           ps_g[ds % 2][:], ALU.add, _sem=sV)
        b.vector(v_evac)
        es2.close()
        return sV.n

    def emit_attn(self, cfg, dr, vbuf, xln_ready):
        nc, b = self.nc, self.block
        sT, sV, sS, sD = self.sT, self.sV, self.sS, self.sD
        n_q, n_kv = cfg["n_q"], cfg["n_kv"]
        tag = dr["tag"]
        es2 = ExitStack()
        k_sb = es2.enter_context(nc.sbuf_tensor(f"atk{tag}", [128, n_kv * T], BF16, side="right"))
        wq_sb = es2.enter_context(nc.sbuf_tensor(f"atwq{tag}", [128, NCH * 128 * n_q], BF16, side="right"))
        wk_sb = es2.enter_context(nc.sbuf_tensor(f"atwk{tag}", [128, NCH * 128 * n_kv], BF16, side="right"))
        wv_sb = es2.enter_context(nc.sbuf_tensor(f"atwv{tag}", [128, NCH * 128 * n_kv], BF16, side="right"))
        bqk_sb = es2.enter_context(nc.sbuf_tensor(f"atbqk{tag}", [128, n_q + n_kv], F32, side="right"))
        bvb = es2.enter_context(nc.sbuf_tensor(f"atbvb{tag}", [128, 128 * n_kv], F32, side="right"))
        bvrow = es2.enter_context(nc.sbuf_tensor(f"atbvr{tag}", [1, 128 * n_kv], BF16, side="right"))
        xgb = [es2.enter_context(nc.sbuf_tensor(f"atxg{i}{tag}", [128, 2 * TC], BF16, side="right"))
               for i in range(2)]
        qstg = [es2.enter_context(nc.sbuf_tensor(f"atqs{i}{tag}", [128, n_q * TC], BF16, side="right"))
                for i in range(2)]
        if cfg["rope"]:
            rxv = self.xres[:].bitcast(BF16)
            ropec = rxv[:, 0:T]
            ropes = rxv[:, T:2 * T]
            rtmp2 = es2.enter_context(nc.sbuf_tensor(f"atr2{tag}", [128, 512], F32, side="right"))
            qb16 = [es2.enter_context(nc.sbuf_tensor(f"atqb{i}{tag}", [128, (n_q + n_kv) * TC], BF16, side="right"))
                    for i in range(2)]
        if cfg["causal"]:
            masks = es2.enter_context(nc.sbuf_tensor(f"atmk{tag}", [128, 4 * 512], BF16, side="right"))
        es = self.es
        sC = Sem(es, nc, f"sC{tag}")
        sWq = Sem(es, nc, f"sWq{tag}")
        sX = [Sem(es, nc, f"sX{i}{tag}") for i in range(2)]
        sQt = [Sem(es, nc, f"sQt{i}{tag}") for i in range(2)]
        sDo = [Sem(es, nc, f"sDo{i}{tag}") for i in range(2)]
        sDq = [Sem(es, nc, f"sDq{i}{tag}") for i in range(2)]
        sOb = Sem(es, nc, f"sOb{tag}")
        sWo = [Sem(es, nc, f"sWo{i}{tag}") for i in range(2)]

        sDa0 = sD.n

        def sy_ag(sync):
            sync.wait_ge(sV.h, xln_ready)
            sD.inc(sync.dma_start(
                out=dr["ag_in"].rearrange("(c p) t -> p c t", p=128),
                in_=self.xln[:].rearrange("p (c t) -> p c t", c=NCH)), 16)
            if cfg["rope"]:
                sD.inc(sync.dma_start(
                    out=dr["xsp"].rearrange("(c p) t -> p c t", p=128),
                    in_=self.xres[:].rearrange("p (c t) -> p c t", c=NCH)), 16)
                sync.wait_ge(sD.h, sD.n)
                sD.inc(sync.dma_start(out=ropec, in_=dr["rope_c"]), 16)
                sD.inc(sync.dma_start(out=ropes, in_=dr["rope_s"]), 16)
            sD.inc(sync.dma_start(out=bqk_sb[:], in_=dr["bqk"]), 16)
            if cfg["causal"]:
                sD.inc(sync.dma_start(out=masks[:], in_=dr["masks"]), 16)
        b.sync(sy_ag)
        consts_ld = sD.n
        assert consts_ld == sD.n

        def g_w(g):
            g.wait_ge(sV.h, xln_ready)
            sWq.inc(g.dma_start(out=bvrow[:], in_=dr["bv"]), 16)
            sWq.inc(g.dma_start(out=wq_sb[:].rearrange("p (c f) -> p c f", c=NCH),
                                in_=dr["wq"].rearrange("(c p) f -> p c f", p=128)), 16)
            sWq.inc(g.dma_start(out=wk_sb[:].rearrange("p (c f) -> p c f", c=NCH),
                                in_=dr["wk"].rearrange("(c p) f -> p c f", p=128)), 16)
            sWq.inc(g.dma_start(out=wv_sb[:].rearrange("p (c f) -> p c f", c=NCH),
                                in_=dr["wv"].rearrange("(c p) f -> p c f", p=128)), 16)
            g.wait_ge(sD.h, consts_ld)
            sC.inc(g.collective_compute(
                "AllGather", mybir.AluOpType.bypass, replica_groups=RG,
                ins=[dr["ag_in"]], outs=[dr["xg"]]), 1)
        b.gpsimd(g_w)
        wq_ld, ag1_done = sWq.n, sC.n

        es_bv = ExitStack()
        ps_bv = es_bv.enter_context(nc.psum_tensor(f"atpsbv{tag}", [128, 128 * n_kv], F32))
        tb0 = sT.n

        def t_bv(t):
            t.wait_ge(sD.h, consts_ld)
            t.wait_ge(sWq.h, wq_ld)
            sT.inc(t.matmul(ps_bv[:], self.ones_1r[:], bvrow[:], start=True, stop=True))
        b.tensor(t_bv)

        def v_bv(v):
            v.wait_ge(sT.h, tb0 + 1)
            self.V(v, "tensor_copy", bvb[:], ps_bv[:], _sem=sV)
        b.vector(v_bv)
        bv_done = sV.n
        es_bv.close()

        es_qk = ExitStack()
        psQ = [es_qk.enter_context(nc.psum_tensor(f"atpq{i}{tag}", [128, TC], F32)) for i in range(n_q)]
        psK = [es_qk.enter_context(nc.psum_tensor(f"atpk{i}{tag}", [128, TC], F32)) for i in range(n_kv)]

        def xg_count(n):
            return ((n // 2) + 1) * 16

        marks = {}

        rope = cfg["rope"]
        nh3 = n_q + n_kv
        if rope:
            psSw = es_qk.enter_context(nc.psum_tensor(f"atsw{tag}", [128, TC], F32))
        mpr = 4 if rope else 1

        def t_qkproj(t):
            t.wait_ge(sC.h, ag1_done)
            t.wait_ge(sWq.h, wq_ld)
            t.wait_ge(sV.h, bv_done)
            for r in range(8):
                if r >= 1:
                    t.wait_ge(sV.h, bv_done + (r - 1) * mpr + 1)
                for mb in range(8):
                    n = r * 8 + mb
                    t.wait_ge(sX[n % 2].h, xg_count(n))
                    ins = None
                    for m2 in range(2):
                        m = mb * 2 + m2
                        rx = xgb[n % 2][:, m2 * TC:(m2 + 1) * TC]
                        for hs in range(n_q):
                            for th in range(2):
                                ins = t.matmul(psQ[hs][:, th * 512:(th + 1) * 512],
                                               wq_sb[:, (m * n_q + hs) * 128:(m * n_q + hs + 1) * 128],
                                               rx[:, th * 512:(th + 1) * 512],
                                               start=(m == 0), stop=(m == NCH - 1))
                        for kv in range(n_kv):
                            for th in range(2):
                                ins = t.matmul(psK[kv][:, th * 512:(th + 1) * 512],
                                               wk_sb[:, (m * n_kv + kv) * 128:(m * n_kv + kv + 1) * 128],
                                               rx[:, th * 512:(th + 1) * 512],
                                               start=(m == 0), stop=(m == NCH - 1))
                    sT.inc(ins)
                    marks[("m", n)] = sT.n
                marks[("qk", r)] = sT.n
                if rope:
                    for j in range(nh3):
                        t.wait_ge(sV.h, bv_done + r * mpr + 1 + j)
                        for th in range(2):
                            ins = t.matmul(psSw[:, th * 512:(th + 1) * 512], self.swp[:],
                                           qb16[r % 2][:, j * TC + th * 512: j * TC + (th + 1) * 512],
                                           start=True, stop=True)
                        sT.inc(ins)
                        marks[("sw", r, j)] = sT.n
        b.tensor(t_qkproj)

        def v_qkev(v):
            for r in range(8):
                v.wait_ge(sT.h, marks[("qk", r)])
                if r >= 2:
                    v.wait_ge(sDq[r % 2].h, ((r - 2) // 2 + 1) * 16)
                if not rope:
                    for hs in range(n_q):
                        self.V(v, "tensor_scalar", qstg[r % 2][:, hs * TC:(hs + 1) * TC],
                               psQ[hs][:], bqk_sb[:, hs:hs + 1], cfg["q_scale"],
                               ALU.add, ALU.mult)
                    for kv in range(n_kv):
                        last = (kv == n_kv - 1)
                        self.V(v, "tensor_scalar", k_sb[:, kv * T + r * TC: kv * T + (r + 1) * TC],
                               psK[kv][:], bqk_sb[:, n_q + kv:n_q + kv + 1], cfg["k_scale"],
                               ALU.add, ALU.mult, _sem=sV if last else None)
                else:
                    for hs in range(n_q):
                        self.V(v, "tensor_scalar_mul", qb16[r % 2][:, hs * TC:(hs + 1) * TC],
                               psQ[hs][:], cfg["q_scale"])
                    self.V(v, "tensor_scalar_mul", qb16[r % 2][:, n_q * TC:(n_q + 1) * TC],
                           psK[0][:], cfg["k_scale"], _sem=sV)
                    for j in range(nh3):
                        v.wait_ge(sT.h, marks[("sw", r, j)])
                        dst = (qstg[r % 2][:, j * TC:(j + 1) * TC] if j < n_q
                               else k_sb[:, (j - n_q) * T + r * TC: (j - n_q) * T + (r + 1) * TC])
                        for th in range(2):
                            hsl = slice(th * 512, (th + 1) * 512)
                            self.V(v, "tensor_mul", self.t1[:],
                                   qb16[r % 2][:, j * TC + th * 512: j * TC + (th + 1) * 512],
                                   ropec[:, r * TC + th * 512: r * TC + (th + 1) * 512])
                            self.V(v, "tensor_mul", rtmp2[:], psSw[:, hsl],
                                   ropes[:, r * TC + th * 512: r * TC + (th + 1) * 512])
                            self.V(v, "tensor_add", dst[:, hsl], self.t1[:], rtmp2[:],
                                   _sem=sV if th == 1 else None)
                marks[("ev", r)] = sV.n
        b.vector(v_qkev)

        es_qk.close()

        es_v = ExitStack()
        psV = [es_v.enter_context(nc.psum_tensor(f"atpv{i}{tag}", [128, 128 * n_kv], F32))
               for i in range(8)]

        vev_base = sV.n

        def t_vproj(t):
            t.wait_ge(sV.h, bv_done + 8 * mpr)
            for r in range(8):
                if r >= 1:
                    t.wait_ge(sV.h, vev_base + r)
                ins = None
                for mb in range(8):
                    n = 64 + r * 8 + mb
                    t.wait_ge(sX[n % 2].h, xg_count(n))
                    for m2 in range(2):
                        m = mb * 2 + m2
                        for tc in range(8):
                            lx = xgb[n % 2][:, m2 * TC + tc * 128: m2 * TC + (tc + 1) * 128]
                            ins = t.matmul(psV[tc][:], lx,
                                           wv_sb[:, m * 128 * n_kv:(m + 1) * 128 * n_kv],
                                           start=(m == 0), stop=(m == NCH - 1))
                    sT.inc(ins)
                    marks[("m", n)] = sT.n
                marks[("v", r)] = sT.n
        b.tensor(t_vproj)

        def v_vev(v):
            for r in range(8):
                v.wait_ge(sT.h, marks[("v", r)])
                for tc in range(8):
                    kc = r * 8 + tc
                    for kv in range(n_kv):
                        self.V(v, "tensor_tensor",
                               vbuf[:, kv * T + kc * 128: kv * T + (kc + 1) * 128],
                               psV[tc][:, kv * 128:(kv + 1) * 128],
                               bvb[:, kv * 128:(kv + 1) * 128], ALU.add,
                               _sem=sV if (tc == 7 and kv == n_kv - 1) else None)
                marks[("vev", r)] = sV.n
        b.vector(v_vev)
        vproj_done = sV.n

        def sy_xg(sync):
            def spill(r):
                sync.wait_ge(sV.h, marks[("ev", r)])
                sDq[r % 2].inc(sync.dma_start(
                    out=dr["qsp"].rearrange("(h p) t -> p h t", p=128)[:, :, r * TC:(r + 1) * TC],
                    in_=qstg[r % 2][:].rearrange("p (h t) -> p h t", h=n_q)), 16)
            sync.wait_ge(sC.h, ag1_done)
            for n in range(128):
                if n >= 2:
                    sync.wait_ge(sT.h, marks[("m", n - 2)])
                rr = (n // 8) if n < 64 else ((n - 64) // 8)
                mb = n % 8
                sX[n % 2].inc(sync.dma_start(
                    out=xgb[n % 2][:].rearrange("p (c t) -> p c t", c=2),
                    in_=dr["xg"].rearrange("(r c p) t -> p r c t", r=8, p=128)[:, rr, mb * 2:mb * 2 + 2, :]), 16)
                if n < 64 and n % 8 == 1 and n >= 9:
                    spill(n // 8 - 1)
                if n == 65:
                    spill(7)
        b.sync(sy_xg)
        es_v.close()

        es_f = ExitStack()
        s_ps = [es_f.enter_context(nc.psum_tensor(f"ats{i}{tag}", [128, 512], F32)) for i in range(2)]
        o_ps = [es_f.enter_context(nc.psum_tensor(f"ato{i}{tag}", [128, 512], F32)) for i in range(2)]
        z_ps = [es_f.enter_context(nc.psum_tensor(f"atz{i}{tag}", [1, 512], F32)) for i in range(2)]
        zb_ps = [es_f.enter_context(nc.psum_tensor(f"atzb{i}{tag}", [128, 512], F32)) for i in range(2)]
        qtile = [es_f.enter_context(nc.sbuf_tensor(f"atqt{i}{tag}", [128, 512], BF16, side="right"))
                 for i in range(2)]
        p_sb = [es_f.enter_context(nc.sbuf_tensor(f"atp{i}{tag}", [128, 512], BF16, side="right"))
                for i in range(2)]
        o_sb = [es_f.enter_context(nc.sbuf_tensor(f"atob{i}{tag}", [128, 512], BF16, side="right"))
                for i in range(2)]
        zb_sb = [es_f.enter_context(nc.sbuf_tensor(f"atzs{i}{tag}", [128, 512], F32, side="right"))
                 for i in range(2)]

        sched = []
        for h in range(n_q):
            for qb in range(16):
                if cfg["causal"]:
                    kcs = [(kc, kc - 4 * qb if kc >= 4 * qb else None) for kc in range(4 * qb + 4)]
                else:
                    kcs = [(kc, None) for kc in range(64)]
                sched.append((h, qb, kcs))
        nf = len(sched)
        ssc = {}
        c = sS.n
        for fi, (h, qb, kcs) in enumerate(sched):
            for kc, msk in kcs:
                c += 1
                ssc[(fi, kc)] = c
        svc_mask, svc_rzb, svc_fin = {}, {}, {}
        c = sV.n
        for fi, (h, qb, kcs) in enumerate(sched):
            for kc, msk in kcs:
                if msk is not None:
                    c += 1
                    svc_mask[(fi, kc)] = c
            c += 1
            svc_rzb[fi] = c
            c += 1
            svc_fin[fi] = c
        kvof = [h // max(1, n_q // n_kv) * T for h in range(n_q)]

        def p_ready(t, fi, kc, msk):
            if msk is not None:
                t.wait_ge(sV.h, svc_mask[(fi, kc)])
            else:
                t.wait_ge(sS.h, ssc[(fi, kc)])

        def t_flash(t):
            t.wait_ge(sV.h, vproj_done)
            for fi, (h, qb, kcs) in enumerate(sched):
                par = fi % 2
                t.wait_ge(sQt[par].h, ((fi // 2) + 1) * 16)
                if fi >= 2:
                    t.wait_ge(sV.h, svc_fin[fi - 2])
                ko = kvof[h]
                nk = len(kcs)

                def smm(j):
                    kc = kcs[j][0]
                    sT.inc(t.matmul(s_ps[j % 2][:],
                                    k_sb[:, ko + kc * 128: ko + (kc + 1) * 128],
                                    qtile[par][:], start=True, stop=True))
                    marks[("s", fi, kc)] = sT.n
                smm(0)
                if nk > 1:
                    smm(1)
                for j, (kc, msk) in enumerate(kcs):
                    p_ready(t, fi, kc, msk)
                    t.matmul(o_ps[par][:], vbuf[:, ko + kc * 128: ko + (kc + 1) * 128],
                             p_sb[j % 2][:], start=(j == 0), stop=(j == nk - 1))
                    sT.inc(t.matmul(z_ps[par][:], self.ones_c1[:], p_sb[j % 2][:],
                                    start=(j == 0), stop=(j == nk - 1)))
                    marks[("z", fi, kc)] = sT.n
                    if j + 2 < nk:
                        smm(j + 2)
                t.wait_ge(sV.h, svc_rzb[fi])
                sT.inc(t.matmul(zb_ps[par][:], self.ones_1r[:],
                                self.rbf[:, par * 512:(par + 1) * 512], start=True, stop=True))
                marks[("zb", fi)] = sT.n
        b.tensor(t_flash)

        def s_flash(s):
            for fi, (h, qb, kcs) in enumerate(sched):
                for j, (kc, msk) in enumerate(kcs):
                    s.wait_ge(sT.h, marks[("s", fi, kc)])
                    self.S(s, "activation", p_sb[j % 2][:], s_ps[j % 2][:], AF.Exp,
                           _sem=self.sS)
        b.scalar(s_flash)

        def v_flash(v):
            for fi, (h, qb, kcs) in enumerate(sched):
                par = fi % 2
                for j, (kc, msk) in enumerate(kcs):
                    if msk is not None:
                        v.wait_ge(sS.h, ssc[(fi, kc)])
                        self.V(v, "tensor_mul", p_sb[j % 2][:], p_sb[j % 2][:],
                               masks[:, msk * 512:(msk + 1) * 512], _sem=sV)
                v.wait_ge(sT.h, marks[("z", fi, kcs[-1][0])])
                self.V(v, "reciprocal", self.rowA[:, par * 512:(par + 1) * 512], z_ps[par][:])
                self.V(v, "tensor_copy", self.rbf[:, par * 512:(par + 1) * 512],
                       self.rowA[:, par * 512:(par + 1) * 512], _sem=sV)
                v.wait_ge(sT.h, marks[("zb", fi)])
                if fi >= 2:
                    v.wait_ge(sDo[par].h, ((fi - 2) // 2 + 1) * 16)
                self.V(v, "tensor_copy", zb_sb[par][:], zb_ps[par][:])
                self.V(v, "tensor_mul", o_sb[par][:], o_ps[par][:], zb_sb[par][:], _sem=sV)
        b.vector(v_flash)

        def sy_flash(sync):
            for i in range(2):
                sync.wait_ge(sDq[i].h, sDq[i].n)
            for ld in range(min(2, nf)):
                h2, qb2, _ = sched[ld]
                sQt[ld % 2].inc(sync.dma_start(
                    out=qtile[ld % 2][:],
                    in_=dr["qsp"].rearrange("(h p) t -> p h t", p=128)[:, h2, qb2 * 512:(qb2 + 1) * 512]), 16)
            for fi, (h, qb, kcs) in enumerate(sched):
                if fi + 2 < nf:
                    sync.wait_ge(sT.h, marks[("s", fi, kcs[-1][0])])
                    h2, qb2, _ = sched[fi + 2]
                    sQt[fi % 2].inc(sync.dma_start(
                        out=qtile[fi % 2][:],
                        in_=dr["qsp"].rearrange("(h p) t -> p h t", p=128)[:, h2, qb2 * 512:(qb2 + 1) * 512]), 16)
                sync.wait_ge(sV.h, svc_fin[fi])
                sDo[fi % 2].inc(sync.dma_start(
                    out=dr["oin"].rearrange("(h p) t -> p h t", p=128)[:, h, qb * 512:(qb + 1) * 512],
                    in_=o_sb[fi % 2][:]), 16)
        b.sync(sy_flash)
        es_f.close()
        es2.close()

        def g_ag2(g):
            for i in range(2):
                g.wait_ge(sDo[i].h, sDo[i].n)
            sC.inc(g.collective_compute(
                "AllGather", mybir.AluOpType.bypass, replica_groups=RG,
                ins=[dr["oin"]], outs=[dr["oout"]]), 1)
        b.gpsimd(g_ag2)
        ag2_done = sC.n

        es_o = ExitStack()
        obuf = es_o.enter_context(nc.sbuf_tensor(f"atobf{tag}", [128, 16 * TC], BF16, side="right"))
        wob = [es_o.enter_context(nc.sbuf_tensor(f"atwo{i}{tag}", [128, NCH * 512], BF16, side="right"))
               for i in range(2)]
        bob = es_o.enter_context(nc.sbuf_tensor(f"atbo{tag}", [1, D], BF16, side="right"))
        borow = es_o.enter_context(nc.sbuf_tensor(f"atbor{tag}", [1, D], F32, side="right"))
        psO = [es_o.enter_context(nc.psum_tensor(f"atpo{i}{tag}", [128, 512], F32)) for i in range(2)]

        def sy_ob(sync):
            if cfg["rope"]:
                sync.wait_ge(sV.h, marks[("ev", 7)])
                sD.inc(sync.dma_start(
                    out=self.xres[:].rearrange("p (c t) -> p c t", c=NCH),
                    in_=dr["xsp"].rearrange("(c p) t -> p c t", p=128)), 16)
            sync.wait_ge(sC.h, ag2_done)
            pid = nc.partition_id()
            tok0 = pid * TC
            src = dr["oout"].rearrange("(r h p) t -> p r h t", r=8, p=128)
            for rj in range(16):
                r, jq = rj // n_q, rj % n_q
                sOb.inc(sync.dma_start(
                    out=obuf[:, rj * TC:(rj + 1) * TC],
                    in_=src[:, r, jq, bass.ds(tok0, TC)]), 16)
            sD.inc(sync.dma_start(out=borow[:], in_=dr["bo"]), 16)
        b.sync(sy_ob)
        ob_ld = sOb.n
        bo_ld = sD.n

        def v_bo(v):
            v.wait_ge(sD.h, bo_ld)
            self.V(v, "tensor_copy", bob[:], borow[:], _sem=sV)
        b.vector(v_bo)
        bo_cast = sV.n

        tbase = sT.n

        def g_wo(g):
            g.wait_ge(sC.h, ag2_done)
            for ob4 in range(4):
                if ob4 >= 2:
                    g.wait_ge(sT.h, tbase + (ob4 - 1) * 8)
                sWo[ob4 % 2].inc(g.dma_start(
                    out=wob[ob4 % 2][:].rearrange("p (c f) -> p c f", c=NCH),
                    in_=dr["wo"][:, ob4 * 512:(ob4 + 1) * 512].rearrange("(c p) f -> p c f", p=128)), 16)
        b.gpsimd(g_wo)

        vo_base = sV.n

        def t_oproj(t):
            t.wait_ge(sOb.h, ob_ld)
            t.wait_ge(sV.h, bo_cast)
            nacc = 0
            for ob4 in range(4):
                t.wait_ge(sWo[ob4 % 2].h, ((ob4 // 2) + 1) * 16)
                for oc2 in range(4):
                    oc = ob4 * 4 + oc2
                    for th in range(2):
                        pi = nacc % 2
                        if nacc >= 2:
                            t.wait_ge(sV.h, vo_base + nacc - 1)
                        for rj in range(16):
                            t.matmul(psO[pi][:],
                                     wob[ob4 % 2][:, (rj * 4 + oc2) * 128:(rj * 4 + oc2 + 1) * 128],
                                     obuf[:, rj * TC + th * 512: rj * TC + (th + 1) * 512],
                                     start=(rj == 0), stop=False)
                        sT.inc(t.matmul(psO[pi][:], bob[:, oc * 128:(oc + 1) * 128],
                                        self.onesr512[:], start=False, stop=True))
                        marks[("o", nacc)] = sT.n
                        nacc += 1
        b.tensor(t_oproj)

        xrel = sD.n

        def v_oadd(v):
            v.wait_ge(sD.h, xrel)
            nacc = 0
            for ob4 in range(4):
                for oc2 in range(4):
                    oc = ob4 * 4 + oc2
                    for th in range(2):
                        v.wait_ge(sT.h, marks[("o", nacc)])
                        sl = slice(oc * TC + th * 512, oc * TC + (th + 1) * 512)
                        self.V(v, "tensor_tensor", self.xres[:, sl], self.xres[:, sl],
                               psO[nacc % 2][:], ALU.add, _sem=sV)
                        nacc += 1
        b.vector(v_oadd)
        es_o.close()
        return sV.n

    def emit_ffn1(self, dr, xln_ready):
        nc, b = self.nc, self.block
        sT, sV, sS, sD, sW = self.sT, self.sV, self.sS, self.sD, self.sW
        tag = dr["tag"]
        es = self.es
        sU = [Sem(es, nc, f"sU{i}{tag}") for i in range(4)]
        sDd = [Sem(es, nc, f"sDd{i}{tag}") for i in range(2)]
        sO2 = [Sem(es, nc, f"sO2{i}{tag}") for i in range(2)]
        sFa = Sem(es, nc, f"sFa{tag}")
        marks = {}

        def sy_spill(sync):
            sync.wait_ge(sV.h, xln_ready)
            sD.inc(sync.dma_start(
                out=dr["xsp"].rearrange("(c p) t -> p c t", p=128),
                in_=self.xres[:].rearrange("p (c t) -> p c t", c=NCH)), 16)
        b.sync(sy_spill)
        spilled = sD.n
        self.free_xres()

        es2 = ExitStack()
        hid = es2.enter_context(nc.sbuf_tensor(f"f1h{tag}", [128, 64 * TC], BF16, side="right"))
        spare = es2.enter_context(nc.sbuf_tensor(f"f1sp{tag}", [128, TC], BF16, side="right"))[:]
        es_ub = ExitStack()
        wub = [es_ub.enter_context(nc.sbuf_tensor(f"f1w{i}{tag}", [128, NCH * 128], BF16, side="right"))
               for i in range(4)]
        sg = [es_ub.enter_context(nc.sbuf_tensor(f"f1s{i}{tag}", [128, TC], BF16, side="right"))
              for i in range(2)]
        hview = [hid[:, fs * TC:(fs + 1) * TC] for fs in range(64)]

        es_up = ExitStack()
        psG = [es_up.enter_context(nc.psum_tensor(f"f1g{i}{tag}", [128, TC], F32)) for i in range(2)]
        psU = [es_up.enter_context(nc.psum_tensor(f"f1u{i}{tag}", [128, TC], F32)) for i in range(2)]

        sT0u = sT.n
        sS0u = sS.n
        sV0u = sV.n

        def g_up(g):
            g.wait_ge(sV.h, xln_ready)
            g.wait_ge(sD.h, spilled)
            for fs in range(64):
                for mi, wd in ((0, dr["w1t"]), (1, dr["w3t"])):
                    bi = mi * 2 + fs % 2
                    if fs >= 2:
                        g.wait_ge(sT.h, sT0u + fs - 1)
                    sU[bi].inc(g.dma_start(
                        out=wub[bi][:].rearrange("p (c f) -> p c f", c=NCH),
                        in_=wd[:, fs * 128:(fs + 1) * 128].rearrange("(c p) f -> p c f", p=128)), 16)
                    marks[("uw", fs, mi)] = (sU[bi], sU[bi].n)
        b.gpsimd(g_up)

        def t_up(t):
            for fs in range(64):
                for mi in range(2):
                    sm, val = marks[("uw", fs, mi)]
                    t.wait_ge(sm.h, val)
                if fs >= 2:
                    t.wait_ge(sS.h, sS0u + fs - 1)
                    t.wait_ge(sV.h, sV0u + fs - 1)
                par = fs % 2
                ins = None
                for m in range(NCH):
                    for th in range(2):
                        t.matmul(psG[par][:, th * 512:(th + 1) * 512],
                                 wub[par][:, m * 128:(m + 1) * 128],
                                 self.xln[:, m * TC + th * 512: m * TC + (th + 1) * 512],
                                 start=(m == 0), stop=(m == NCH - 1))
                        ins = t.matmul(psU[par][:, th * 512:(th + 1) * 512],
                                       wub[2 + par][:, m * 128:(m + 1) * 128],
                                       self.xln[:, m * TC + th * 512: m * TC + (th + 1) * 512],
                                       start=(m == 0), stop=(m == NCH - 1))
                sT.inc(ins)
        b.tensor(t_up)

        def s_up(s):
            for fs in range(64):
                s.wait_ge(sT.h, sT0u + fs + 1)
                if fs >= 2:
                    s.wait_ge(sV.h, sV0u + fs - 1)
                self.S(s, "activation", sg[fs % 2][:], psG[fs % 2][:], AF.Silu, _sem=sS)
        b.scalar(s_up)

        def v_up(v):
            for fs in range(64):
                v.wait_ge(sS.h, sS0u + fs + 1)
                self.V(v, "tensor_mul", hview[fs], sg[fs % 2][:], psU[fs % 2][:], _sem=sV)
        b.vector(v_up)
        up_done = sV.n
        es_up.close()
        es_ub.close()

        def v_bfly(v):
            nonlocal spare
            for k in range(6):
                for a in range(64):
                    if a & (1 << k):
                        continue
                    bidx = a | (1 << k)
                    last = (k == 5 and a == 31)
                    self.V(v, "tensor_sub", spare, hview[a], hview[bidx])
                    self.V(v, "tensor_add", hview[a], hview[a], hview[bidx],
                           _sem=sV if last else None)
                    hview[bidx], spare = spare, hview[bidx]
        b.vector(v_bfly)
        bfly_done = sV.n

        es_h = ExitStack()
        psH = [es_h.enter_context(nc.psum_tensor(f"f1ph{i}{tag}", [128, TC], F32)) for i in range(2)]
        sT0h = sT.n
        sV0h = sV.n

        def t_h128(t):
            t.wait_ge(sV.h, bfly_done)
            for fc in range(64):
                if fc >= 2:
                    t.wait_ge(sV.h, sV0h + fc - 1)
                ins = None
                for th in range(2):
                    ins = t.matmul(psH[fc % 2][:, th * 512:(th + 1) * 512], self.h128[:],
                                   hview[fc][:, th * 512:(th + 1) * 512], start=True, stop=True)
                sT.inc(ins)
        b.tensor(t_h128)

        def v_h128(v):
            for fc in range(64):
                v.wait_ge(sT.h, sT0h + fc + 1)
                self.V(v, "tensor_copy", hview[fc], psH[fc % 2][:], _sem=sV)
        b.vector(v_h128)
        h128_done = sV.n
        es_h.close()

        es_d = ExitStack()
        wdb = [es_d.enter_context(nc.sbuf_tensor(f"f1d{i}{tag}", [128, 16 * 128], BF16, side="right"))
               for i in range(2)]
        ostg = [es_d.enter_context(nc.sbuf_tensor(f"f1o0{tag}", [128, TC], F32, side="right"))] * 2
        psO = [es_d.enter_context(nc.psum_tensor(f"f1po{i}{tag}", [128, TC], F32)) for i in range(2)]
        sT0d = sT.n
        sV0d = sV.n

        def g_dn(g):
            g.wait_ge(sV.h, h128_done)
            for ds in range(16):
                for hf in range(4):
                    bi = (ds * 4 + hf) % 2
                    ldi = ds * 4 + hf
                    if ldi >= 2:
                        g.wait_ge(sT.h, sT0d + ldi - 1)
                    sDd[bi].inc(g.dma_start(
                        out=wdb[bi][:].rearrange("p (c f) -> p c f", c=16),
                        in_=dr["w2t"][hf * 2048:(hf + 1) * 2048, ds * 128:(ds + 1) * 128]
                            .rearrange("(c p) f -> p c f", p=128)), 16)
                    marks[("dw", ds, hf)] = (sDd[bi], sDd[bi].n)
        b.gpsimd(g_dn)

        def t_dn(t):
            for ds in range(16):
                par = ds % 2
                if ds >= 2:
                    t.wait_ge(sV.h, sV0d + ds - 1)
                ins = None
                for hf in range(4):
                    sm, val = marks[("dw", ds, hf)]
                    t.wait_ge(sm.h, val)
                    for fc in range(16):
                        fcg = hf * 16 + fc
                        for th in range(2):
                            ins = t.matmul(psO[par][:, th * 512:(th + 1) * 512],
                                           wdb[(ds * 4 + hf) % 2][:, fc * 128:(fc + 1) * 128],
                                           hview[fcg][:, th * 512:(th + 1) * 512],
                                           start=(hf == 0 and fc == 0),
                                           stop=(hf == 3 and fc == 15))
                    sT.inc(ins)
        b.tensor(t_dn)

        def v_dn(v):
            for ds in range(16):
                v.wait_ge(sT.h, sT0d + ds * 4 + 4)
                if ds >= 1:
                    v.wait_ge(sO2[(ds - 1) % 2].h, ((ds - 1) // 2 + 1) * 16)
                self.V(v, "tensor_copy", ostg[ds % 2][:], psO[ds % 2][:], _sem=sV)
        b.vector(v_dn)

        def sy_ostg(sync):
            for ds in range(16):
                sync.wait_ge(sV.h, sV0d + ds + 1)
                sO2[ds % 2].inc(sync.dma_start(
                    out=dr["fst"][ds * 128:(ds + 1) * 128, :], in_=ostg[ds % 2][:]), 16)
        b.sync(sy_ostg)
        es_d.close()
        es2.close()

        self.alloc_xres()

        def sy_rst(sync):
            for i in range(2):
                sync.wait_ge(sO2[i].h, sO2[i].n)
            sD.inc(sync.dma_start(
                out=self.xres[:].rearrange("p (c t) -> p c t", c=NCH),
                in_=dr["xsp"].rearrange("(c p) t -> p c t", p=128)), 16)
        b.sync(sy_rst)
        rst = sD.n

        def g_acc(g):
            g.wait_ge(sD.h, rst)
            sFa.inc(g.dma_start(
                out=self.xres[:].rearrange("p (c t) -> p c t", c=NCH),
                in_=dr["fst"].rearrange("(c p) t -> p c t", p=128),
                accum_op=ALU.add), 16)
        b.gpsimd(g_acc)
        acc_done = sFa.n

        def v_fence(v):
            v.wait_ge(sFa.h, acc_done)
            self.V(v, "engine_nop", _sem=sV)
        b.vector(v_fence)
        return sV.n



def build_moe_standalone(race_check=True, stage=3):
    bld = Builder(race_check)
    nc = bld.nc
    x_in = bld.dram_in("x", [D, TC])
    wr_d = bld.dram_in("Wr", [D, 8])
    wg_d = bld.dram_in("Wg", [8, D, D])
    wu_d = bld.dram_in("Wu", [8, D, D])
    wd_d = bld.dram_in("Wd", [8, D, D])
    x_out = bld.dram_out("xout", [D, TC])
    bld.setup()
    with ExitStack() as es_ph:
        hbuf = es_ph.enter_context(nc.sbuf_tensor("hbuf", [128, NCH * TC], BF16, side="right"))
        bld.load_consts()
        bld.load_x(x_in)
        xln_ready = bld.emit_ln(hbuf, wait_sems=[(bld.sD, bld.sD.n)])
        if stage >= 2:
            done = bld.emit_moe(wr_d, wg_d, wu_d, wd_d, hbuf, xln_ready,
                                n_experts=(8 if stage >= 3 else 0))
        else:
            done = xln_ready
        bld.store_x(x_out, wait=[(bld.sV, done)])
    bld.finish()
    return bld


def build_attn_standalone(l1=False, race_check=True):
    bld = Builder(race_check)
    nc = bld.nc
    n_q, n_kv = 2, (1 if l1 else 2)
    cfg = dict(n_q=n_q, n_kv=n_kv, rope=l1, causal=l1,
               q_scale=(128.0 ** -0.25 if l1 else 128.0 ** -0.5),
               k_scale=(128.0 ** -0.25 if l1 else 1.0))
    x_in = bld.dram_in("x", [D, TC])
    dr = dict(tag="a1" if l1 else "a0")
    dr["wq"] = bld.dram_in("wq", [D, 128 * n_q])
    dr["wk"] = bld.dram_in("wk", [D, 128 * n_kv])
    dr["wv"] = bld.dram_in("wv", [D, 128 * n_kv])
    dr["bqk"] = bld.dram_in("bqk", [128, n_q + n_kv])
    dr["bv"] = bld.dram_in("bv", [1, 128 * n_kv])
    dr["wo"] = bld.dram_in("wo", [D, D])
    dr["bo"] = bld.dram_in("bo", [1, D])
    if l1:
        dr["xsp"] = bld.dram_int("xsp", [D, TC])
        dr["rope_c"] = bld.dram_in("rope_c", [128, T], BF16)
        dr["rope_s"] = bld.dram_in("rope_s", [128, T], BF16)
        dr["masks"] = bld.dram_in("masks", [128, 4 * 512], BF16)
    dr["ag_in"] = bld.dram_int("ag_in", [D, TC], BF16)
    dr["xg"] = bld.dram_int("xg", [8 * D, TC], BF16)
    dr["qsp"] = bld.dram_int("qsp", [n_q * 128, T], BF16)
    dr["oin"] = bld.dram_int("oin", [n_q * 128, T], BF16)
    dr["oout"] = bld.dram_int("oout", [8 * n_q * 128, T], BF16)
    x_out = bld.dram_out("xout", [D, TC])
    bld.setup()
    with ExitStack() as es_ph:
        scr = es_ph.enter_context(nc.sbuf_tensor("lnscr", [128, NCH * TC], BF16, side="right"))
        bld.load_consts()
        bld.load_x(x_in)
        xln_ready = bld.emit_ln(scr, wait_sems=[(bld.sD, bld.sD.n)])
    done = bld.emit_attn(cfg, dr, bld.xln[:], xln_ready)
    bld.store_x(x_out, wait=[(bld.sV, done)])
    bld.finish()
    return bld


def build_ffn1_standalone(race_check=True):
    bld = Builder(race_check)
    nc = bld.nc
    x_in = bld.dram_in("x", [D, TC])
    dr = dict(tag="f1")
    dr["w1t"] = bld.dram_in("w1t", [D, 4 * D])
    dr["w3t"] = bld.dram_in("w3t", [D, 4 * D])
    dr["w2t"] = bld.dram_in("w2t", [4 * D, D])
    dr["xsp"] = bld.dram_int("xspf", [D, TC])
    dr["fst"] = bld.dram_int("fst", [D, TC])
    x_out = bld.dram_out("xout", [D, TC])
    bld.setup()
    with ExitStack() as es_ph:
        scr = es_ph.enter_context(nc.sbuf_tensor("lnscr", [128, NCH * TC], BF16, side="right"))
        bld.load_consts()
        bld.load_x(x_in)
        xln_ready = bld.emit_ln(scr, wait_sems=[(bld.sD, bld.sD.n)])
    done = bld.emit_ffn1(dr, xln_ready)
    bld.store_x(x_out, wait=[(bld.sV, done)])
    bld.finish()
    return bld


_prog_cache = {}


def _cached(name, fn):
    if name not in _prog_cache:
        _prog_cache[name] = fn()
    return _prog_cache[name]


def kernel(**inputs):
    g = {k: np.asarray(v, np.float32) for k, v in inputs.items()}
    consts = host_consts()
    rc, rs = host_rope()
    mks = host_masks()
    x = np.ascontiguousarray(g["L0_attn_x"][0].T)
    xs = [np.ascontiguousarray(x[:, c * TC:(c + 1) * TC]) for c in range(NCORE)]

    moe_b = _cached("moe", build_moe_standalone)
    at0_b = _cached("a0", lambda: build_attn_standalone(l1=False))
    f1_b = _cached("f1", build_ffn1_standalone)
    at1_b = _cached("a1", lambda: build_attn_standalone(l1=True))

    def run(bld, per_core):
        ims = []
        for c in range(NCORE):
            im = dict(consts)
            im["x"] = xs[c]
            im.update(per_core[c])
            ims.append(im)
        res = run_bass_kernel_spmd(bld.nc, ims, core_ids=list(range(NCORE)))
        return [np.asarray(res.results[c]["xout"]) for c in range(NCORE)]

    moe_w = [dict(Wr=g["L0_ffn_Wr"], Wg=g["L0_ffn_Wg"], Wu=g["L0_ffn_Wu"], Wd=g["L0_ffn_Wd"])
             for _ in range(NCORE)]

    wqkv, bqkv = g["L0_attn_w_qkv"], g["L0_attn_b_qkv"]
    at0_w = []
    for c in range(NCORE):
        hs = slice(2 * c * 128, (2 * c + 2) * 128)
        bq = bqkv[0:D][hs].reshape(2, 128).T
        bk = bqkv[D:2 * D][hs].reshape(2, 128).T
        at0_w.append(dict(
            wq=np.ascontiguousarray(wqkv[:, 0:D][:, hs]),
            wk=np.ascontiguousarray(wqkv[:, D:2 * D][:, hs]),
            wv=np.ascontiguousarray(wqkv[:, 2 * D:3 * D][:, hs]),
            bqk=np.ascontiguousarray(np.concatenate([bq, bk], 1), dtype=np.float32),
            bv=np.ascontiguousarray(bqkv[2 * D:][hs][None, :]),
            wo=g["L0_attn_w_out"], bo=g["L0_attn_b_out"][None, :]))

    f1_w = [dict(w1t=np.ascontiguousarray(g["L1_ffn_W1"].T),
                 w3t=np.ascontiguousarray(g["L1_ffn_W3"].T),
                 w2t=np.ascontiguousarray(g["L1_ffn_W2"].T)) for _ in range(NCORE)]

    waq, wak, wav, wao = g["L1_attn_w_aq"], g["L1_attn_w_ak"], g["L1_attn_w_av"], g["L1_attn_w_ao"]
    zer_bqk = np.zeros((128, 3), np.float32)
    zer_bv = np.zeros((1, 128), np.float32)
    zer_bo = np.zeros((1, D), np.float32)
    wao_f = np.ascontiguousarray(wao.reshape(16 * 128, D))
    at1_w = []
    for c in range(NCORE):
        gi, h0 = c // 2, (2 * c) % 4
        at1_w.append(dict(
            wq=np.ascontiguousarray(np.concatenate([waq[gi, h0], waq[gi, h0 + 1]], 1)),
            wk=np.ascontiguousarray(wak[gi]), wv=np.ascontiguousarray(wav[gi]),
            bqk=zer_bqk, bv=zer_bv, wo=wao_f, bo=zer_bo,
            rope_c=rc, rope_s=rs, masks=mks))

    for bld, w in ((moe_b, moe_w), (at0_b, at0_w), (moe_b, moe_w),
                   (f1_b, f1_w), (at1_b, at1_w), (f1_b, f1_w)):
        xs = run(bld, w)

    out = np.concatenate(xs, 1).T
    return out.reshape(1, T, D).astype(np.float32)
